# version 03
# baseline (speedup 1.0000x reference)
import os
from contextlib import ExitStack

import numpy as np
import ml_dtypes

import concourse.bass as bass
import concourse.mybir as mybir
from concourse.bass_utils import run_bass_kernel_spmd

F32 = mybir.dt.float32
BF16 = mybir.dt.bfloat16
AF = mybir.ActivationFunctionType
ALU = mybir.AluOpType

HS = 2048
NK = 16
NV = 32
SEQ = 4096
CH = 64
TT = 512
NT = SEQ // TT
NPC = TT // 128
KT = HS // 128
OTQ = 64
QW = 515
EPS_LN = 1e-5
EPS_L2 = 1e-6
EPS_RMS = 1e-6
RSQ_DK = float(128) ** -0.5
LN_RSQ_DK = float(np.log(128.0) * -0.5)


class P:
    def __init__(self, nc, debug=()):
        self.nc = nc
        self.debug = {}
        self.debug_req = set(debug)
        self.semval = {}
        self.sems = {}
        self.ctx = ExitStack()

    def sem(self, name):
        if name not in self.sems:
            self.sems[name] = self.ctx.enter_context(self.nc.semaphore(name))
            self.semval[name] = 0
        return self.sems[name]

    def inc(self, instr, name, n=1):
        self.sem(name)
        self.semval[name] += n
        instr.then_inc(self.sems[name], n)
        return instr

    def wait(self, eng, name, val=None):
        self.sem(name)
        eng.wait_ge(self.sems[name], self.semval[name] if val is None else val)

    def val(self, name):
        self.sem(name)
        return self.semval[name]

    def sb(self, name, shape, dtype):
        t = self.ctx.enter_context(self.nc.sbuf_tensor(name, shape, dtype))
        setattr(self, name, t)
        return t

    def psum(self, name, shape, dtype=F32):
        t = self.ctx.enter_context(self.nc.psum_tensor(name, shape, dtype))
        setattr(self, name, t)
        return t


def _bc(ap, axis, n):
    shp = list(ap.shape)
    new = shp[:axis + 1] + [n] + shp[axis + 1:]
    return ap.unsqueeze(axis + 1).broadcast_to(new)


def _kslot(kdt, par, bb):
    a = kdt[:][par * 64:par * 64 + 64, 0:2048] \
        .rearrange("pp (s d) -> pp s d", s=16)
    return a[:, 8 * bb + par: 8 * bb + par + 7:2, :]


def _sc4(scc, col0, g, par):
    sl = scc[par * 64:(par + 1) * 64,
             col0 + 8 * g + par: col0 + 8 * g + par + 7:2]
    return sl.unsqueeze(2).broadcast_to([64, 4, 128])


def build_core_program(debug=(), tiles=NT, stop=None):
    nc = bass.Bass("TRN2", target_bir_lowering=False)
    p = P(nc, debug)

    din = lambda n, s, d=F32: nc.dram_tensor(n, list(s), d, kind="ExternalInput").ap()
    xT = din("xT", [HS, SEQ])
    wqkv = din("wqkv", [128, 131072])
    wz = din("wz", [128, 65536])
    wb_i = din("wb", [HS, NV])
    wa_i = din("wa", [HS, NV])
    wout = din("wout", [128, 65536])
    wgu = din("wgu", [128, 262144])
    wdn = din("wdn", [128, 131072])
    convw = din("convw", [128, OTQ * 4])
    dtb = din("dtb", [1, NV])
    alog = din("alog", [1, NV])
    nrmw = din("nrmw", [128, 1])
    cones = din("cones", [128, 1], BF16)
    conesr = din("conesr", [1, 128], BF16)
    cband_i = din("cbandi", [128, 63], BF16)
    cid128 = din("cid128", [128, 128], BF16)
    cid128f = din("cid128f", [128, 128], F32)
    cid16f = din("cid16f", [64, 16], F32)
    cibd = din("cibd", [128, 128], BF16)
    cLbd = din("cLbd", [128, 128], F32)
    cm_incl = din("cm_incl", [128, 64], BF16)
    cm_strictneg = din("cm_strictneg", [128, 64], BF16)
    cmT_neg = din("cmT_neg", [128, 64], BF16)
    cmT_incl = din("cmT_incl", [128, 64], BF16)

    outT = nc.dram_tensor("outT", [HS, SEQ], F32, kind="ExternalOutput").ap()

    dsc = lambda n, s, d: nc.dram_tensor(n, list(s), d, kind="Internal").ap()
    swqkv = dsc("swqkv", [128, 131072], BF16)
    swz = dsc("swz", [128, 65536], BF16)
    swb = dsc("swb", [HS, NV], BF16)
    swa = dsc("swa", [HS, NV], BF16)
    swout = dsc("swout", [128, 65536], BF16)
    swgu = dsc("swgu", [128, 262144], BF16)
    swdn = dsc("swdn", [128, 131072], BF16)
    sdtb = dsc("sdtb", [1, NV], BF16)
    drow = dsc("drow", [1, 1024], F32)
    dgbc = dsc("dgbc", [8, 2048], F32)
    dscb = dsc("dscb", [8, 4096], BF16)
    dglbuf = dsc("dglbuf", [8, NV], F32)
    deglr = dsc("deglr", [8, NV], F32)
    drms = dsc("drms", [NV, TT], BF16)
    dalr = dsc("dalr", [1, NV], F32)

    sb = p.sb
    sb("hln", [128, KT * TT], BF16)
    sb("qkvp", [128, OTQ * QW], BF16)
    sb("tails", [128, OTQ * 3], BF16)
    sb("core", [128, NV * TT], BF16)
    sb("Sst", [128, NV * 128], F32)
    sb("Sbf", [128, NV * 128], BF16)
    sb("xscr", [128, 4096], BF16)
    sb("xbuf", [128, KT * TT], F32)
    sb("chA", [128, 16 * 128], BF16)
    sb("chB", [128, 16 * 128], BF16)
    sb("chQ", [128, 16 * 128], BF16)
    sb("rv", [128, 16 * 128], BF16)
    sb("wbuf", [128, 3 * 1024], BF16)
    sb("sdtbt", [1, NV], BF16)
    sb("convwt", [128, OTQ * 4], F32)
    sb("nrmwt", [128, 1], F32)
    sb("alogt", [1, 2 * NV], F32)
    sb("nealb", [128, NV], F32)
    sb("cmask", [128, 64 * 4], BF16)
    sb("cidb", [128, 128 * 2], BF16)
    sb("cidf", [128, 128 * 2], F32)
    sb("cid16", [64, 16], F32)
    sb("conest", [128, 1], BF16)
    sb("conesrt", [1, 128], BF16)
    sb("cband", [128, 63], BF16)
    for nm in ["btok", "gcum", "bskt", "kwsc", "kdsc"]:
        sb(nm, [128, NPC * NV], F32)
    sb("sktok", [128, NPC * NK], F32)
    sb("sqtok", [128, NPC * NK], F32)
    sb("eglbb", [128, NV], F32)
    sb("fcon", [128, 8], F32)
    sb("ctmp", [128, 2 * TT], BF16)
    for _i, _v in enumerate([EPS_LN, EPS_L2, -1.0, 1.0 / 128, -0.5, LN_RSQ_DK]):
        nc.const_aps.register(F32, _v, p.fcon[:, _i:_i + 1])
    p.rowst = p.qkvp[:].bitcast(F32)[0:1, 0:1792]
    p.rbt = p.qkvp[:, 3584:4608]
    coreF = p.core[:].bitcast(F32)
    p.sqkq = coreF[0:16, 0:512]
    p.sqkk = coreF[0:16, 896:1408]
    p.gtok = coreF[:, 512:640]
    p.glbt = coreF[:, 640:768]
    p.qet = coreF[:, 768:896]
    p.tmp512 = p.xscr[:].bitcast(F32)[:, 0:512]
    p.epos = p.xscr[:].bitcast(F32)[:, 128:256]
    p.eneg = p.xscr[:].bitcast(F32)[:, 256:384]
    p.eglr = p.xscr[:].bitcast(F32)[0:1, 512:1024]
    p.ghm = p.xscr[:].bitcast(F32)[0:32, 1024:1536]
    p.wsml = p.wbuf[:, 0:KT * NV * 2]
    p.kdt = p.chB

    xbF = p.xbuf[:].bitcast(F32)
    xbB = p.xbuf[:].bitcast(BF16)
    gcol_v = xbF[:, 0:2048]
    decay_v = xbB[:, 4096:8192]
    attnT2_v = xbB[:, 4096:8192]
    aall_v = xbB[:, 8192:10240]
    attnT_v = xbB[:, 10240:12288]
    skb_v = xbB[:, 12288:13312]
    sqb_v = xbB[:, 13312:14336]
    scc_v = xbB[:, 12288:13312].bitcast(F32)[:, 0:96]
    qeb_v = xbB[:, 14336:16384]
    dT_v = p.xscr[:].bitcast(F32)[:, 0:2048]

    pgk = p.psum("pgk", [128, 1024])
    pgq = p.psum("pgq", [128, 1024])
    pb4 = p.psum("pb4", [128, 512])
    pb5 = p.psum("pb5", [128, 512])
    pb6 = p.psum("pb6", [128, 512])
    pb7 = p.psum("pb7", [128, 512])
    pA = [pgk[:, 0:512], pgk[:, 512:1024], pgq[:, 0:512], pgq[:, 512:1024]]
    pB = [pb4[:], pb5[:], pb6[:], pb7[:]]

    def ln_stats(blk, srcF, tag):
        s_mm, s_cast, s_fin, s_row = tag + "m", tag + "c", tag + "f", tag + "r"
        b_mm = p.val(s_mm)
        b_cast = p.val(s_cast)
        b_fin = p.val(s_fin)

        @blk.vector
        def _(e):
            for h in range(2):
                if h == 1:
                    p.wait(e, s_mm, b_mm + 16)
                for j in range(8):
                    p.inc(e.tensor_copy(p.xscr[:, j * 512:(j + 1) * 512],
                                        srcF(h * 8 + j)), s_cast, 1)
                p.wait(e, s_mm, b_mm + 8 + 16 * h)
                p.inc(e.tensor_mul(p.xscr[:], p.xscr[:], p.xscr[:]), s_cast, 1)
            p.wait(e, s_mm, b_mm + 32)
            r = p.rowst
            e.tensor_scalar_mul(r[0:1, 0:512], pA[2][0:1, 0:512], 1.0 / HS)
            e.tensor_scalar_mul(r[0:1, 1024:1536], pA[3][0:1, 0:512], 1.0 / HS)
            e.tensor_mul(r[0:1, 512:1024], r[0:1, 0:512], r[0:1, 0:512])
            p.inc(e.tensor_sub(r[0:1, 1024:1536], r[0:1, 1024:1536],
                               r[0:1, 512:1024]), s_fin, 1)

        @blk.tensor
        def _(e):
            for h in range(2):
                p.wait(e, s_cast, b_cast + h * 9 + 8)
                for j in range(8):
                    kt = h * 8 + j
                    p.inc(e.matmul(pA[2][0:1, 0:512], p.conest[:],
                                   p.xscr[:, j * 512:(j + 1) * 512],
                                   start=(kt == 0), stop=(kt == 15)), s_mm, 1)
                p.wait(e, s_cast, b_cast + h * 9 + 9)
                for j in range(8):
                    kt = h * 8 + j
                    p.inc(e.matmul(pA[3][0:1, 0:512], p.conest[:],
                                   p.xscr[:, j * 512:(j + 1) * 512],
                                   start=(kt == 0), stop=(kt == 15)), s_mm, 1)

        @blk.scalar
        def _(e):
            p.wait(e, s_fin, b_fin + 1)
            e.activation(p.rowst[0:1, 1024:1536], p.rowst[0:1, 1024:1536],
                         AF.Ln, bias=EPS_LN)
            p.inc(e.activation(p.rowst[0:1, 512:1024], p.rowst[0:1, 1024:1536],
                               AF.Exp, scale=-0.5), s_fin, 1)

        @blk.sync
        def _(e):
            p.wait(e, s_fin, b_fin + 2)
            p.inc(e.dma_start(out=drow, in_=p.rowst[0:1, 0:1024]), s_row, 16)

        @blk.gpsimd
        def _(e):
            p.wait(e, s_row)
            p.inc(e.dma_start(out=p.rbt[:].rearrange("pp (a s) -> pp a s", a=2),
                              in_=drow.rearrange("o (a s) -> o a s", a=2)
                              .broadcast_to([128, 2, 512])), s_row + "g", 16)
            e.wait_ge(p.sem(s_row + "g"), p.val(s_row + "g"))

    def ln_norm(blk, srcF, dstF, tag):
        s_n = tag + "n"

        @blk.vector
        def _(e):
            for kt in range(16):
                e.tensor_sub(p.tmp512[:], srcF(kt), p.rbt[:, 0:512])
                p.inc(e.tensor_mul(dstF(kt), p.tmp512[:], p.rbt[:, 512:1024]),
                      s_n, 1)

    import os as _os
    def stream_proj(blk, wdram, n_ot, n_kt, rhsF, evac_fn, tag, evac_eng):
        s_mr, s_ev = tag + "r", tag + "e"
        s_wl = [tag + "w%d" % s for s in range(3)]
        nround = (n_kt + 7) // 8
        b_wl = [p.val(s) for s in s_wl]
        b_mr = p.val(s_mr)
        b_ev = p.val(s_ev)

        @blk.sync
        def _(e):
            for ot in range(n_ot):
                for r in range(nround):
                    idx = nround * ot + r
                    slot = idx % 3
                    if idx >= 3:
                        p.wait(e, s_mr, b_mr + idx - 2)
                    p.inc(e.dma_start(
                        out=p.wbuf[:, slot * 1024:(slot + 1) * 1024],
                        in_=wdram[:, idx * 1024:(idx + 1) * 1024]),
                        s_wl[slot], 16)

        @blk.tensor
        def _(e):
            for ot in range(n_ot):
                bank = pA[ot % 2]
                if ot >= 2:
                    p.wait(e, s_ev, b_ev + ot - 1)
                for r in range(nround):
                    idx = nround * ot + r
                    slot = idx % 3
                    p.wait(e, s_wl[slot], b_wl[slot] + 16 * (idx // 3 + 1))
                    kth = 8
                    for j in range(kth):
                        kt = r * 8 + j
                        ins = e.matmul(bank,
                                       p.wbuf[:, slot * 1024 + j * 128:
                                              slot * 1024 + (j + 1) * 128],
                                       rhsF(kt),
                                       start=(kt == 0), stop=(kt == n_kt - 1))
                        if j == kth - 1:
                            p.inc(ins, s_mr, 1)

        eng = getattr(blk, evac_eng)

        @eng
        def _(e):
            for ot in range(n_ot):
                p.wait(e, s_mr, b_mr + nround * (ot + 1))
                p.inc(evac_fn(e, ot, pA[ot % 2]), s_ev, 1)

    with p.ctx:
        with nc.Block() as blk:
            @blk.sync
            def _(e):
                for dst, src in [
                        (p.conest[:], cones), (p.conesrt[:], conesr),
                        (p.cband[:], cband_i), (p.nrmwt[:], nrmw),
                        (p.convwt[:], convw),
                        (p.cidb[:, 0:128], cid128), (p.cidb[:, 128:256], cibd),
                        (p.cidf[:, 0:128], cid128f), (p.cidf[:, 128:256], cLbd),
                        (p.cid16[:], cid16f),
                        (p.cmask[:, 0:64], cm_incl),
                        (p.cmask[:, 64:128], cm_strictneg),
                        (p.cmask[:, 128:192], cmT_neg),
                        (p.cmask[:, 192:256], cmT_incl),
                        (p.alogt[0:1, NV:2 * NV], alog)]:
                    p.inc(e.dma_start(out=dst, in_=src), "cld", 16)
                e.wait_ge(p.sem("cld"), p.val("cld"))

            @blk.gpsimd
            def _(e):
                for dst, src in [(swqkv, wqkv), (swz, wz), (swb, wb_i),
                                 (swa, wa_i), (swgu, wgu), (swdn, wdn),
                                 (swout, wout), (sdtb, dtb)]:
                    p.inc(e.dma_start(out=dst, in_=src), "cvt", 16)
                e.wait_ge(p.sem("cvt"), p.val("cvt"))
                p.inc(e.dma_start(out=p.sdtbt[:], in_=sdtb), "cvt", 16)
                e.wait_ge(p.sem("cvt"), p.val("cvt"))

            @blk.vector
            def _(e):
                for _i, _v in enumerate([EPS_LN, EPS_L2, -1.0, 1.0 / 128, -0.5, LN_RSQ_DK]):
                    e.memset(p.fcon[:, _i:_i + 1], _v)
                e.memset(p.tails[:], 0.0)
                e.memset(p.chA[:], 0.0)
                e.memset(p.chB[:], 0.0)
                e.memset(p.chQ[:], 0.0)
                e.memset(p.Sst[:], 0.0)
                e.memset(p.Sbf[:], 0.0)

        with nc.Block() as blk:
            @blk.scalar
            def _(e):
                p.inc(e.activation(p.alogt[0:1, 0:NV], p.alogt[0:1, NV:2 * NV],
                                   AF.Exp), "alg", 1)

            @blk.vector
            def _(e):
                p.wait(e, "alg", 1)
                p.inc(e.tensor_scalar_mul(p.alogt[0:1, 0:NV],
                                          p.alogt[0:1, 0:NV], -1.0), "alg", 1)

            @blk.sync
            def _(e):
                p.wait(e, "alg", 2)
                p.inc(e.dma_start(out=dalr, in_=p.alogt[0:1, 0:NV]), "algd", 16)
                p.wait(e, "algd")
                p.inc(e.dma_start(out=p.nealb[:],
                                  in_=dalr.partition_broadcast(128)), "algd", 16)
                e.wait_ge(p.sem("algd"), p.val("algd"))

        for t in range(tiles):
            t0 = t * TT

            with nc.Block() as blk:
                @blk.sync
                def _(e):
                    src = xT[:, t0:t0 + TT].rearrange("(kt pp) s -> pp kt s", pp=128)
                    p.inc(e.dma_start(
                        out=p.xbuf[:].rearrange("pp (kt s) -> pp kt s", kt=16),
                        in_=src), "xl", 16)
                    e.wait_ge(p.sem("xl"), p.val("xl"))

            with nc.Block() as blk:
                ln_stats(blk, lambda kt: xbF[:, kt * 512:(kt + 1) * 512], "ln")
            with nc.Block() as blk:
                ln_norm(blk, lambda kt: xbF[:, kt * 512:(kt + 1) * 512],
                        lambda kt: p.hln[:, kt * 512:(kt + 1) * 512], "lnn")

            with nc.Block() as blk:
                def evq(e, ot, bank):
                    return e.copy(p.qkvp[:, ot * QW + 3: ot * QW + QW], bank)
                stream_proj(blk, swqkv, OTQ, KT,
                            lambda kt: p.hln[:, kt * 512:(kt + 1) * 512],
                            evq, "sp", "scalar")

            _conv_block(p, nc)
            _ba_block(p, nc, swb, swa, pB)
            _gcum_block(p, nc, dgbc, dglbuf, deglr, pB)
            _l2_block(p, nc, dscb, pB)
            _dump(p, nc, t, [
                ("hln", p.hln), ("qkvs", p.qkvp), ("btok", p.btok),
                ("gcum", p.gcum), ("sqkq", p.sqkq),
                ("sqkk", p.sqkk), ("epos", p.epos),
                ("eneg", p.eneg), ("bskt", p.bskt), ("kwsc", p.kwsc),
                ("kdsc", p.kdsc)])

            if stop == "A":
                break
            for cp in range(NPC):
                _chunk_pair(p, nc, cp, dgbc, dscb, deglr,
                            pA, pB, gcol_v, decay_v, aall_v, attnT_v,
                            attnT2_v, skb_v, sqb_v, qeb_v, dT_v, scc_v)
                if stop is not None and stop.startswith("B") and len(stop) > 1 \
                        and cp >= int(stop[1]):
                    break
            if stop is not None and stop.startswith("B"):
                break

            _dump(p, nc, t, [("coreo", p.core), ("Sst", p.Sst)])

            _phaseC(p, nc, t, t0, xT, swz, swout, drms, pA, pB,
                    stream_proj, ln_stats, xbF)
            _dump(p, nc, t, [("x1", p.xbuf)])
            if stop == "C":
                break

            _phaseD(p, nc, t, t0, outT, swgu, swdn, pA, pB,
                    stream_proj, ln_stats, ln_norm, xbF)

        if stop is not None:
            with nc.Block() as blk:
                @blk.sync
                def _(e):
                    dst = outT[:, 0:TT].rearrange("(kt pp) s -> pp kt s", pp=128)
                    p.inc(e.dma_start(
                        out=dst,
                        in_=p.xbuf[:].rearrange("pp (kt s) -> pp kt s", kt=16)),
                        "outd", 16)
                    e.wait_ge(p.sem("outd"), p.val("outd"))
        with nc.Block() as blk:
            @blk.sync
            def _(e):
                p.wait(e, "outd")

    return nc, p


def _dump(p, nc, t, items):
    todo = []
    for it in items:
        n, sbuf = it[0], it[1]
        name = n + f"_{t}"
        if name in p.debug_req:
            todo.append((name, sbuf))
    if not todo:
        return
    with nc.Block() as blk:
        @blk.sync
        def _(e):
            for name, sbuf in todo:
                shape = [int(v) for v in sbuf.shape]
                dt = sbuf.dtype if hasattr(sbuf, "dtype") else F32
                ap = p.nc.dram_tensor("dbg_" + name, shape, dt,
                                      kind="ExternalOutput").ap()
                p.debug[name] = ap
                p.inc(e.dma_start(out=ap, in_=sbuf[:]), "dbgd", 16)
            e.wait_ge(p.sem("dbgd"), p.val("dbgd"))


def _conv_block(p, nc):
    with nc.Block() as blk:
        b_sil = p.val("sil")
        b_cnv = p.val("cnv")

        @blk.vector
        def _(e):
            qv = p.qkvp[:].rearrange("pp (o w) -> pp o w", w=QW)
            tv = p.tails[:].rearrange("pp (o w) -> pp o w", w=3)
            e.tensor_copy(qv[:, :, 0:3], tv)
            e.tensor_copy(tv, qv[:, :, 512:515])
            for ot in range(OTQ):
                half = (ot % 2) * TT
                q = p.qkvp[:, ot * QW: ot * QW + QW]
                if ot >= 2:
                    p.wait(e, "sil", b_sil + ot - 1)
                e.tensor_scalar_mul(p.ctmp[:, half:half + 512], q[:, 0:512],
                                    p.convwt[:, ot * 4:ot * 4 + 1])
                for j in range(1, 4):
                    ins = e.scalar_tensor_tensor(
                        p.ctmp[:, half:half + 512], q[:, j:j + 512],
                        p.convwt[:, ot * 4 + j:ot * 4 + j + 1],
                        p.ctmp[:, half:half + 512], ALU.mult, ALU.add)
                    if j == 3:
                        p.inc(ins, "cnv", 1)

        @blk.scalar
        def _(e):
            for ot in range(OTQ):
                half = (ot % 2) * TT
                p.wait(e, "cnv", b_cnv + ot + 1)
                p.inc(e.activation(p.qkvp[:, ot * QW + 3: ot * QW + QW],
                                   p.ctmp[:, half:half + 512], AF.Silu),
                      "sil", 1)


def _ba_block(p, nc, swb, swa, pB):
    with nc.Block() as blk:
        b_bam = p.val("bam")
        b_spl = p.val("spl")

        @blk.sync
        def _(e):
            p.inc(e.dma_start(
                out=p.wsml[:, 0:KT * NV].rearrange("pp (kk c) -> pp kk c", kk=KT),
                in_=swb.rearrange("(kk pp) c -> pp kk c", pp=128)), "wbl", 16)
            p.inc(e.dma_start(
                out=p.wsml[:, KT * NV:2 * KT * NV].rearrange("pp (kk c) -> pp kk c", kk=KT),
                in_=swa.rearrange("(kk pp) c -> pp kk c", pp=128)), "wbl", 16)
            e.wait_ge(p.sem("wbl"), p.val("wbl"))

        @blk.tensor
        def _(e):
            p.wait(e, "wbl")
            for q4 in range(4):
                for kt in range(KT):
                    lhs = p.hln[:, kt * 512 + q4 * 128: kt * 512 + (q4 + 1) * 128]
                    e.matmul(pB[0][:, q4 * 32:(q4 + 1) * 32], lhs,
                             p.wsml[:, kt * 32:(kt + 1) * 32],
                             start=(kt == 0), stop=(kt == KT - 1))
                for kt in range(KT):
                    lhs = p.hln[:, kt * 512 + q4 * 128: kt * 512 + (q4 + 1) * 128]
                    e.matmul(pB[0][:, 128 + q4 * 32: 128 + (q4 + 1) * 32], lhs,
                             p.wsml[:, KT * NV + kt * 32: KT * NV + (kt + 1) * 32],
                             start=(kt == 0), stop=False)
                p.inc(e.matmul(pB[0][:, 128 + q4 * 32: 128 + (q4 + 1) * 32],
                               p.conesrt[:], p.sdtbt[:],
                               start=False, stop=True), "bam", 1)

        @blk.scalar
        def _(e):
            p.wait(e, "bam", b_bam + 4)
            p.inc(e.activation(p.btok[:], pB[0][:, 0:128], AF.Sigmoid), "bok", 1)
            e.activation(p.tmp512[:, 0:128], pB[0][:, 128:256], AF.Abs)
            e.activation(p.tmp512[:, 128:256], p.tmp512[:, 0:128], AF.Exp,
                         scale=-1.0)
            p.inc(e.activation(p.tmp512[:, 256:384], p.tmp512[:, 128:256],
                               AF.Ln, bias=1.0), "spl", 1)

        @blk.vector
        def _(e):
            p.wait(e, "spl", b_spl + 1)
            e.tensor_scalar_max(p.tmp512[:, 384:512], pB[0][:, 128:256], 0.0)
            e.tensor_add(p.tmp512[:, 256:384], p.tmp512[:, 256:384],
                         p.tmp512[:, 384:512])
            p.inc(e.tensor_tensor(
                p.gtok[:].rearrange("pp (a v) -> pp a v", a=4),
                p.tmp512[:, 256:384].rearrange("pp (a v) -> pp a v", a=4),
                _bc(p.nealb[:], 0, 4), ALU.mult), "gok", 1)


def _gcum_block(p, nc, dgbc, dglbuf, deglr, pB):
    with nc.Block() as blk:
        b_gcm = p.val("gcm")
        b_ghm = p.val("ghm")

        @blk.tensor
        def _(e):
            p.wait(e, "gok")
            for q4 in range(4):
                p.inc(e.matmul(pB[1][:, q4 * 32:(q4 + 1) * 32],
                               p.cidf[:, 128:256],
                               p.gtok[:, q4 * 32:(q4 + 1) * 32],
                               start=True, stop=True), "gcm", 1)

        @blk.vector
        def _(e):
            p.wait(e, "gcm", b_gcm + 4)
            p.inc(e.tensor_copy(p.gcum[:], pB[1][:, 0:128]), "gcv", 1)

        @blk.tensor
        def _(e):
            p.wait(e, "gcv")
            for q4 in range(4):
                p.inc(e.matmul(pB[2][0:32, q4 * 128:(q4 + 1) * 128],
                               p.gcum[:, q4 * 32:(q4 + 1) * 32],
                               p.cidf[:, 0:128], start=True, stop=True),
                      "ghm", 1)

        @blk.vector
        def _(e):
            p.wait(e, "ghm", b_ghm + 4)
            p.inc(e.tensor_copy(p.ghm, pB[2][0:32, 0:512]), "ghv", 1)

        @blk.sync
        def _(e):
            p.wait(e, "ghv")
            p.inc(e.dma_start(
                out=dgbc.rearrange("ch (v t) -> v ch t", v=NV),
                in_=p.ghm.rearrange("v (ch t) -> v ch t", ch=8)), "gdm", 16)
            for j in range(2):
                p.inc(e.dma_start(
                    out=dglbuf.rearrange("(q j) v -> j q v", q=4)[j:j + 1],
                    in_=p.gcum[63 + 64 * j: 64 + 64 * j, :]
                        .rearrange("o (q v) -> o q v", q=4)), "gdm", 16)
            p.wait(e, "gdm")
            for j in range(2):
                p.inc(e.dma_start(
                    out=p.glbt[j * 64:(j + 1) * 64, :]
                        .rearrange("pp (q v) -> pp q v", q=4),
                    in_=dglbuf.rearrange("(q j) v -> j q v", q=4)[j]
                        .unsqueeze(0).broadcast_to([64, 4, NV])), "glb", 16)
            e.wait_ge(p.sem("glb"), p.val("glb"))

        @blk.scalar
        def _(e):
            p.wait(e, "glb")
            p.inc(e.activation(p.epos[:], p.gcum[:], AF.Exp), "ep1", 1)

        @blk.vector
        def _(e):
            p.wait(e, "ep1")
            p.inc(e.tensor_sub(p.tmp512[:, 0:128], p.glbt[:], p.gcum[:]),
                  "en1", 1)

        @blk.scalar
        def _(e):
            p.wait(e, "en1")
            p.inc(e.activation(p.eneg[:], p.tmp512[:, 0:128], AF.Exp), "en2", 1)

        @blk.sync
        def _(e):
            p.inc(e.dma_start(out=p.eglr[0:1, 0:256],
                              in_=dglbuf.rearrange("a v -> (a v)").unsqueeze(0)),
                  "eg1", 16)

        @blk.scalar
        def _(e):
            p.wait(e, "eg1")
            p.inc(e.activation(p.eglr[0:1, 256:512], p.eglr[0:1, 0:256],
                               AF.Exp), "eg1", 1)

        @blk.sync
        def _(e):
            p.wait(e, "eg1")
            p.inc(e.dma_start(
                out=deglr.rearrange("a v -> (a v)").unsqueeze(0),
                in_=p.eglr[0:1, 256:512]), "eg2", 16)
            e.wait_ge(p.sem("eg2"), p.val("eg2"))


def _l2_block(p, nc, dscb, pB):
    with nc.Block() as blk:
        b_l2m = p.val("l2m")
        b_l2s = p.val("l2s")
        b_stm = p.val("stm")
        b_qem = p.val("qem")

        @blk.vector
        def _(e):
            for i in range(32):
                half = (i % 2) * TT
                q = p.qkvp[:, i * QW + 3: i * QW + QW]
                if i >= 2:
                    p.wait(e, "l2m", b_l2m + i - 1)
                p.inc(e.tensor_mul(p.ctmp[:, half:half + 512], q, q), "l2s", 1)

        @blk.tensor
        def _(e):
            for i in range(32):
                half = (i % 2) * TT
                p.wait(e, "l2s", b_l2s + i + 1)
                grp, h = i // 16, i % 16
                bank = pB[2 + grp]
                p.inc(e.matmul(bank[0:16, 0:512],
                               p.cband[:, 31 - h:47 - h],
                               p.ctmp[:, half:half + 512],
                               start=(h == 0), stop=(h == 15)), "l2m", 1)

        @blk.scalar
        def _(e):
            p.wait(e, "l2m", b_l2m + 32)
            e.activation(p.sqkq, pB[2][0:16, 0:512], AF.Ln, bias=EPS_L2)
            e.activation(p.sqkq, p.sqkq, AF.Exp, scale=-0.5, bias=LN_RSQ_DK)
            e.activation(p.sqkk, pB[3][0:16, 0:512], AF.Ln, bias=EPS_L2)
            p.inc(e.activation(p.sqkk, p.sqkk, AF.Exp, scale=-0.5), "l2r", 1)

        @blk.tensor
        def _(e):
            p.wait(e, "l2r")
            for q4 in range(4):
                e.matmul(pB[0][:, q4 * 16:(q4 + 1) * 16],
                         p.sqkk[:, q4 * 128:(q4 + 1) * 128],
                         p.cid16[0:16, :], start=True, stop=True)
                p.inc(e.matmul(pB[0][:, 64 + q4 * 16: 64 + (q4 + 1) * 16],
                               p.sqkq[:, q4 * 128:(q4 + 1) * 128],
                               p.cid16[0:16, :], start=True, stop=True), "stm", 1)

        @blk.vector
        def _(e):
            p.wait(e, "stm", b_stm + 4)
            e.tensor_copy(p.sktok[:], pB[0][:, 0:64])
            p.inc(e.tensor_copy(p.sqtok[:], pB[0][:, 64:128]), "stv", 1)
            skr = _bc(p.sktok[:].rearrange("pp (q h) -> pp q h", q=4), 2, 2)
            sqr = _bc(p.sqtok[:].rearrange("pp (q h) -> pp q h", q=4), 2, 2)
            b4 = lambda ap: ap.rearrange("pp (q v) -> pp q v", q=4) \
                              .rearrange("pp q (h r) -> pp q h r", r=2)
            e.tensor_tensor(b4(p.bskt[:]), b4(p.btok[:]), skr, ALU.mult)
            e.tensor_mul(p.kwsc[:], p.bskt[:], p.epos[:])
            e.tensor_tensor(b4(p.kdsc[:]), b4(p.eneg[:]), skr, ALU.mult)
            p.inc(e.tensor_tensor(b4(p.qet[:]), b4(p.epos[:]), sqr, ALU.mult),
                  "scp", 1)

        @blk.tensor
        def _(e):
            p.wait(e, "scp")
            for q4 in range(4):
                p.inc(e.matmul(pB[1][0:32, q4 * 128:(q4 + 1) * 128],
                               p.qet[:, q4 * 32:(q4 + 1) * 32],
                               p.cidf[:, 0:128], start=True, stop=True),
                      "qem", 1)

        @blk.vector
        def _(e):
            p.wait(e, "qem", b_qem + 4)
            p.inc(e.tensor_copy(p.ghm, pB[1][0:32, 0:512]),
                  "qev", 1)

        @blk.gpsimd
        def _(e):
            p.wait(e, "qev")
            p.inc(e.dma_start(
                out=dscb[:, 2048:4096].rearrange("ch (v t) -> v ch t", v=NV),
                in_=p.ghm.rearrange("v (ch t) -> v ch t", ch=8)), "qed", 16)
            p.wait(e, "l2r")
            p.inc(e.dma_start(
                out=dscb[:, 1024:2048].rearrange("ch (h t) -> h ch t", h=NK),
                in_=p.sqkq.rearrange("h (ch t) -> h ch t", ch=8)), "qed", 16)
            p.inc(e.dma_start(
                out=dscb[:, 0:1024].rearrange("ch (h t) -> h ch t", h=NK),
                in_=p.sqkk.rearrange("h (ch t) -> h ch t", ch=8)), "qed", 16)
            e.wait_ge(p.sem("qed"), p.val("qed"))


def _chunk_pair(p, nc, cp, dgbc, dscb, deglr, pA, pB,
                gcol_v, decay_v, aall_v, attnT_v, attnT2_v, skb_v, sqb_v,
                qeb_v, dT_v, scc_v):
    tok0 = cp * 128
    qsl = lambda h2, c: p.qkvp[:, h2 * QW + 3 + tok0 + c * 64:
                               h2 * QW + 3 + tok0 + c * 64 + 64]
    ksl = lambda h2, c: p.qkvp[:, (16 + h2) * QW + 3 + tok0 + c * 64:
                               (16 + h2) * QW + 3 + tok0 + c * 64 + 64]
    vsl = lambda v, c: p.qkvp[:, (32 + v) * QW + 3 + tok0 + c * 64:
                              (32 + v) * QW + 3 + tok0 + c * 64 + 64]

    with nc.Block() as blk:
        @blk.sync
        def _(e):
            for c in range(2):
                ch = 2 * cp + c
                p.inc(e.dma_start(
                    out=gcol_v[c * 64:(c + 1) * 64, :],
                    in_=dgbc[ch:ch + 1, :].partition_broadcast(64)), "pgc", 16)
                p.inc(e.dma_start(
                    out=skb_v[c * 64:(c + 1) * 64, :],
                    in_=dscb[ch:ch + 1, 0:1024].partition_broadcast(64)),
                    "pgc", 16)
                p.inc(e.dma_start(
                    out=sqb_v[c * 64:(c + 1) * 64, :],
                    in_=dscb[ch:ch + 1, 1024:2048].partition_broadcast(64)),
                    "pgc", 16)
            e.wait_ge(p.sem("pgc"), p.val("pgc"))

        @blk.tensor
        def _(e):
            for c in range(2):
                for h2 in range(NK):
                    bank = pA[h2 // 8]
                    p.inc(e.matmul(bank[c * 64:(c + 1) * 64,
                                        (h2 % 8) * 64:(h2 % 8 + 1) * 64],
                                   ksl(h2, c), ksl(h2, c),
                                   start=True, stop=True), "pgk", 1)
            for c in range(2):
                for h2 in range(NK):
                    bank = pA[2 + h2 // 8]
                    p.inc(e.matmul(bank[c * 64:(c + 1) * 64,
                                        (h2 % 8) * 64:(h2 % 8 + 1) * 64],
                                   ksl(h2, c), qsl(h2, c),
                                   start=True, stop=True), "pgq", 1)

    with nc.Block() as blk:
        pk = _bc(p.pgk[:].rearrange("pp (h tj) -> pp h tj", h=16), 1, 2)
        pq = _bc(p.pgq[:].rearrange("pp (h tj) -> pp h tj", h=16), 1, 2)
        skrep = _bc(skb_v.rearrange("pp (h tj) -> pp h tj", h=NK), 1, 2)
        sqrep = _bc(sqb_v.rearrange("pp (h tj) -> pp h tj", h=NK), 1, 2)
        m_incl = _bc(p.cmask[:, 0:64], 0, NV)
        m_sneg = _bc(p.cmask[:, 64:128], 0, NV)
        m_Tneg = _bc(p.cmask[:, 128:192], 0, NV)
        m_Tinc = _bc(p.cmask[:, 192:256], 0, NV)
        grow = _bc(p.gcum[:, cp * 32:(cp + 1) * 32], 1, 64)
        browA = _bc(p.bskt[:, cp * 32:(cp + 1) * 32], 1, 64) \
            .rearrange("pp (h r) tj -> pp h r tj", r=2)
        skrow = _bc(_bc(p.sktok[:, cp * 16:(cp + 1) * 16], 1, 2), 2, 64)
        a3 = lambda ap: ap.rearrange("pp (v tj) -> pp v tj", v=NV)
        a4 = lambda ap: ap.rearrange("pp (h r tj) -> pp h r tj", h=16, r=2)
        b_asm = p.val("asm")
        b_exp = p.val("exp")

        @blk.vector
        def _(e):
            e.tensor_tensor(a3(gcol_v), grow, a3(gcol_v), ALU.subtract)
            p.inc(e.tensor_tensor(a3(dT_v), a3(gcol_v), m_Tneg, ALU.mult),
                  "asm", 1)
            p.inc(e.tensor_tensor(a3(gcol_v), a3(gcol_v), m_incl, ALU.mult),
                  "asm", 1)

        @blk.scalar
        def _(e):
            p.wait(e, "asm", b_asm + 2)
            p.inc(e.activation(decay_v[:, 0:2048], gcol_v, AF.Exp), "exp", 1)
            p.inc(e.activation(decay_v[:, 2048:4096], dT_v, AF.Exp), "exp", 1)

        @blk.vector
        def _(e):
            p.wait(e, "exp", b_exp + 2)
            dA = decay_v[:, 0:2048]
            dTt = decay_v[:, 2048:4096]
            e.tensor_tensor(a4(dA), a4(dA), skrep, ALU.mult)
            e.tensor_tensor(a3(dA), a3(dA), m_sneg, ALU.mult)
            e.tensor_tensor(a4(aall_v), pk, browA, ALU.mult)
            p.inc(e.tensor_mul(aall_v, aall_v, dA), "aok", 1)
            e.tensor_tensor(a4(dTt), a4(dTt), sqrep, ALU.mult)
            e.tensor_tensor(a3(dTt), a3(dTt), m_Tinc, ALU.mult)
            e.tensor_tensor(a4(attnT_v), pq, skrow, ALU.mult)
            p.inc(e.tensor_mul(attnT_v, attnT_v, dTt), "tok", 1)

    with nc.Block() as blk:
        @blk.vector
        def _(e):
            p.inc(e.memset(attnT2_v, 0.0), "at2m", 1)

        @blk.sync
        def _(e):
            p.wait(e, "at2m")
            for c in range(2):
                for par in range(2):
                    p.inc(e.dma_start(
                        out=attnT2_v[par * 64:(par + 1) * 64, c * 2048:(c + 1) * 2048]
                            .rearrange("pp (v ti) -> pp v ti", v=NV)[:, par::2, :],
                        in_=attnT_v[c * 64:(c + 1) * 64, :]
                            .rearrange("pp (v ti) -> pp v ti", v=NV)[:, par::2, :]),
                        "at2", 16)
            e.wait_ge(p.sem("at2"), p.val("at2"))

    import os as _os
    if _os.environ.get("MK_SCANLVL") == "0":
        return
    for c in range(2):
        ch = 2 * cp + c
        _chunk_scan(p, nc, cp, c, ch, deglr, dscb, pA, pB, aall_v, attnT2_v,
                    qeb_v, scc_v, qsl, ksl, vsl)


def _chunk_scan(p, nc, cp, c, ch, deglr, dscb, pA, pB, aall_v, attnT2_v,
                qeb_v, scc_v, qsl, ksl, vsl):
    tok0 = cp * 128
    idb = p.cidb[:, 0:128]

    with nc.Block() as blk:
        b_pvm = p.val("pvm")
        b_pkm = p.val("pkm")

        @blk.sync
        def _(e):
            p.inc(e.dma_start(out=p.eglbb[:],
                              in_=deglr[ch % 8:ch % 8 + 1, :]
                              .partition_broadcast(128)), "egb", 16)
            p.inc(e.dma_start(
                out=qeb_v,
                in_=dscb[ch:ch + 1, 2048:4096].partition_broadcast(128)),
                "qbc", 16)
            for ci, tile in enumerate([p.kwsc, p.btok, p.kdsc]):
                srcsl = tile[c * 64:(c + 1) * 64, cp * 32:(cp + 1) * 32]
                for half in range(2):
                    p.inc(e.dma_start(
                        out=scc_v[half * 64:(half + 1) * 64,
                                  ci * 32:(ci + 1) * 32],
                        in_=srcsl), "qbc", 16)
            for m in range(2):
                p.inc(e.dma_start(
                    out=p.chA[64 * m:64 * m + 64, :]
                        .rearrange("pp (pr x tj) -> pp pr x tj", pr=16, x=2)[:, :, m, :],
                    in_=aall_v[c * 64:(c + 1) * 64, :]
                        .rearrange("pp (v tj) -> pp v tj", v=NV)[:, m::2, :]),
                    "bdc", 16)
            e.wait_ge(p.sem("egb"), p.val("egb"))
            e.wait_ge(p.sem("bdc"), p.val("bdc"))

        @blk.vector
        def _(e):
            p.wait(e, "qbc")
            qv = p.qkvp[:].rearrange("pp (o w) -> pp o w", w=QW)[
                :, 0:16, 3 + tok0 + c * 64: 3 + tok0 + c * 64 + 64]
            qv = _bc(qv, 1, 2)
            qe4 = qeb_v.rearrange("pp (h r tj) -> pp h r tj", h=16, r=2)
            p.inc(e.tensor_tensor(qe4, qv, qe4, ALU.mult), "qeb", 1)

        @blk.tensor
        def _(e):
            for v in range(NV):
                par, pr = v % 2, v // 2
                bank = pA[pr // 4]
                p.inc(e.matmul(bank[par * 64:(par + 1) * 64,
                                    (pr % 4) * 128:(pr % 4 + 1) * 128],
                               vsl(v, c), idb, start=True, stop=True), "pvm", 1)
            for v in range(NV):
                par, pr = v % 2, v // 2
                bank = pB[pr // 4]
                p.inc(e.matmul(bank[par * 64:(par + 1) * 64,
                                    (pr % 4) * 128:(pr % 4 + 1) * 128],
                               ksl(v // 2, c),
                               p.Sbf[:, v * 128:(v + 1) * 128],
                               start=True, stop=True), "pkm", 1)

        @blk.vector
        def _(e):
            p.wait(e, "pvm", b_pvm + 32)
            p.wait(e, "pkm", b_pkm + 32)
            for par in range(2):
                t2 = p.ctmp[par * 64:(par + 1) * 64, 0:512] \
                    .rearrange("pp (j d) -> pp j d", j=4)
                for g in range(4):
                    pv = pA[g][par * 64:(par + 1) * 64, 0:512]
                    pk2 = pB[g][par * 64:(par + 1) * 64, 0:512]
                    rvs = p.rv[par * 64:(par + 1) * 64, g * 512:(g + 1) * 512] \
                        .rearrange("pp (j d) -> pp j d", j=4)
                    e.tensor_tensor(t2, pk2.rearrange("pp (j d) -> pp j d", j=4),
                                    _sc4(scc_v, 0, g, par), ALU.mult)
                    e.tensor_tensor(rvs, pv.rearrange("pp (j d) -> pp j d", j=4),
                                    _sc4(scc_v, 32, g, par), ALU.mult)
                    e.tensor_tensor(rvs, rvs, t2, ALU.subtract)
            p.inc(e.engine_nop(), "rok", 1)

    import os as _os
    if _os.environ.get("MK_SCANLVL") == "1":
        return
    with nc.Block() as blk:
        b_tb0 = p.val("tb0")
        b_tbe = p.val("tbe")
        b_tbq = p.val("tbq")
        b_ta = p.val("ta")
        b_tbs = p.val("tbs")
        b_tq = p.val("tq")
        b_tae = p.val("tae")
        b_tbse = p.val("tbse")
        b_tqe = p.val("tqe")

        @blk.tensor
        def _(e):
            for pr in range(16):
                bank = pA[pr // 4]
                p.inc(e.matmul(bank[:, (pr % 4) * 128:(pr % 4 + 1) * 128],
                               p.chA[:, pr * 128:(pr + 1) * 128], idb,
                               start=True, stop=True), "tb0", 1)
            p.wait(e, "tbe", b_tbe + 4)
            p.wait(e, "tbq", b_tbq + 4)
            for lev in range(5):
                for pr in range(16):
                    p.inc(e.matmul(pA[pr // 4][:, (pr % 4) * 128:(pr % 4 + 1) * 128],
                                   p.chB[:, pr * 128:(pr + 1) * 128],
                                   p.chA[:, pr * 128:(pr + 1) * 128],
                                   start=True, stop=True), "ta", 1)
                if lev < 4:
                    for pr in range(16):
                        p.inc(e.matmul(pB[pr // 4][:, (pr % 4) * 128:(pr % 4 + 1) * 128],
                                       p.chA[:, pr * 128:(pr + 1) * 128],
                                       p.chB[:, pr * 128:(pr + 1) * 128],
                                       start=True, stop=True), "tbs", 1)
                p.wait(e, "tae", b_tae + 4 * (lev + 1))
                for pr in range(16):
                    p.inc(e.matmul(pA[pr // 4][:, (pr % 4) * 128:(pr % 4 + 1) * 128],
                                   p.chA[:, pr * 128:(pr + 1) * 128],
                                   p.chQ[:, pr * 128:(pr + 1) * 128],
                                   start=True, stop=True), "tq", 1)
                if lev < 4:
                    p.wait(e, "tbse", b_tbse + 4 * (lev + 1))
                p.wait(e, "tqe", b_tqe + 4 * (lev + 1))

        @blk.scalar
        def _(e):
            p.wait(e, "tb0", b_tb0 + 16)
            for b in range(4):
                p.inc(e.copy(p.chB[:, b * 512:(b + 1) * 512], pA[b][:]), "tbe", 1)
            for lev in range(5):
                p.wait(e, "ta", b_ta + 16 * (lev + 1))
                for b in range(4):
                    p.inc(e.copy(p.chA[:, b * 512:(b + 1) * 512], pA[b][:]),
                          "tae", 1)

        @blk.vector
        def _(e):
            p.wait(e, "tb0", b_tb0 + 16)
            ib = p.cidb[:, 128:256]
            for b in range(4):
                p.inc(e.tensor_tensor(
                    p.chQ[:, b * 512:(b + 1) * 512]
                    .rearrange("pp (j x) -> pp j x", j=4),
                    pA[b][:].rearrange("pp (j x) -> pp j x", j=4),
                    ib.unsqueeze(1).broadcast_to([128, 4, 128]),
                    ALU.add), "tbq", 1)
            for lev in range(5):
                if lev < 4:
                    p.wait(e, "tbs", b_tbs + 16 * (lev + 1))
                    for b in range(4):
                        p.inc(e.tensor_copy(p.chB[:, b * 512:(b + 1) * 512],
                                            pB[b][:]), "tbse", 1)
                p.wait(e, "tq", b_tq + 16 * (lev + 1))
                for b in range(4):
                    p.inc(e.tensor_add(p.chQ[:, b * 512:(b + 1) * 512],
                                       p.chQ[:, b * 512:(b + 1) * 512],
                                       pA[b][:]), "tqe", 1)

    if _os.environ.get("MK_SCANLVL") == "2":
        return
    _b3 = _os.environ.get("MK_B3", "full")
    _lvl = {"ap": 0, "out": 1, "kd": 2, "ds": 3, "full": 9}[_b3]
    with nc.Block() as blk:
        b_apm = p.val("apm")
        b_ape = p.val("ape")
        b_otm = p.val("otm")
        b_kdm = p.val("kdm")
        b_dsm = p.val("dsm")
        b_dse = p.val("dse")
        b_ote = p.val("ote")
        b_kde = p.val("kde")

        @blk.tensor
        def _(e):
            p.wait(e, "rok")
            for pr in range(16):
                p.inc(e.matmul(pB[pr // 4][:, (pr % 4) * 128:(pr % 4 + 1) * 128],
                               p.chQ[:, pr * 128:(pr + 1) * 128],
                               p.rv[:, pr * 128:(pr + 1) * 128],
                               start=True, stop=True), "apm", 1)
            if _lvl < 1:
                return
            p.wait(e, "ape", b_ape + 4)
            p.wait(e, "qeb")
            for v in range(NV):
                bank = pA[v // 8]
                off = (v % 8) * 64
                e.matmul(bank[:, off:off + 64],
                         p.Sbf[:, v * 128:(v + 1) * 128],
                         qeb_v[:, v * 64:(v + 1) * 64], start=True, stop=False)
                par, pr = v % 2, v // 2
                p.inc(e.matmul(bank[:, off:off + 64],
                               p.rv[:, pr * 128:(pr + 1) * 128],
                               attnT2_v[:, c * 2048 + v * 64:
                                        c * 2048 + (v + 1) * 64],
                               start=False, stop=True), "otm", 1)
            if _lvl < 2:
                return
            for h2 in range(NK):
                bank = pB[h2 // 4]
                for par in range(2):
                    p.inc(e.matmul(
                        bank[par * 64:(par + 1) * 64,
                             (h2 % 4) * 128:(h2 % 4 + 1) * 128],
                        ksl(h2, c), idb, start=True, stop=True),
                        "kdm", 1)
            if _lvl < 3:
                return
            p.wait(e, "ote", b_ote + 4)
            for w in range(2):
                p.wait(e, "kde", b_kde + 4 * (w + 1))
                if w == 1:
                    p.wait(e, "dse", b_dse + 16)
                for i in range(16):
                    v = w * 16 + i
                    par, pr = v % 2, v // 2
                    bank = pA[i // 4]
                    p.inc(e.matmul(bank[:, (i % 4) * 128:(i % 4 + 1) * 128],
                                   p.kdt[:][:, i * 128:(i + 1) * 128],
                                   p.rv[:, pr * 128:(pr + 1) * 128],
                                   start=True, stop=True), "dsm", 1)

        @blk.vector
        def _(e):
            p.wait(e, "apm", b_apm + 16)
            for b in range(4):
                p.inc(e.tensor_copy(p.rv[:, b * 512:(b + 1) * 512], pB[b][:]),
                      "ape", 1)
            if _lvl < 1:
                return
            p.wait(e, "otm", b_otm + 32)
            for b in range(4):
                p.inc(e.tensor_copy(
                    p.core[:].rearrange("pp (v s) -> pp v s", v=NV)[
                        :, b * 8:(b + 1) * 8,
                        cp * 128 + c * 64: cp * 128 + c * 64 + 64],
                    pA[b][:].rearrange("pp (v s) -> pp v s", v=8)), "ote", 1)
            if _lvl < 2:
                return
            p.wait(e, "kdm", b_kdm + 32)
            for w in range(2):
                if w == 1:
                    p.wait(e, "dsm", b_dsm + 16)
                e.memset(p.kdt[:], 0.0)
                for b in (2 * w, 2 * w + 1):
                    bb = b - 2 * w
                    for par in range(2):
                        p.inc(e.tensor_tensor(
                            _kslot(p.kdt, par, bb),
                            pB[b][par * 64:(par + 1) * 64, :]
                            .rearrange("pp (j d) -> pp j d", j=4),
                            _sc4(scc_v, 64, b, par), ALU.mult), "kde", 1)
            if _lvl < 3:
                return
            p.wait(e, "egb")
            for w in range(2):
                p.wait(e, "dsm", b_dsm + 16 * (w + 1))
                for i in range(16):
                    v = w * 16 + i
                    bank = pA[i // 4]
                    p.inc(e.scalar_tensor_tensor(
                        p.Sst[:, v * 128:(v + 1) * 128],
                        p.Sst[:, v * 128:(v + 1) * 128],
                        p.eglbb[:, v:v + 1],
                        bank[:, (i % 4) * 128:(i % 4 + 1) * 128],
                        ALU.mult, ALU.add), "dse", 1)
            p.inc(e.tensor_copy(p.Sbf[:], p.Sst[:]), "sbc", 1)


def _phaseC(p, nc, t, t0, xT, swz, swout, drms, pA, pB,
            stream_proj, ln_stats, xbF):
    qB = p.qkvp[:].bitcast(BF16)
    sqv = qB[:, 0:16384]
    rstdb = qB[:, 0:16384]
    szv = qB[:, 16384:32768]

    with nc.Block() as blk:
        b_rmm = p.val("rmm")

        @blk.vector
        def _(e):
            p.inc(e.tensor_mul(sqv, p.core[:], p.core[:]), "rms", 1)

        @blk.tensor
        def _(e):
            p.wait(e, "rms")
            for v in range(NV):
                p.inc(e.matmul(pB[0][0:32, 0:512],
                               p.cband[:, 31 - v:63 - v],
                               sqv[:, v * 512:(v + 1) * 512],
                               start=(v == 0), stop=(v == NV - 1)), "rmm", 1)

        @blk.scalar
        def _(e):
            p.wait(e, "rmm", b_rmm + 32)
            e.activation(p.ghm, pB[0][0:32, 0:512],
                         AF.Ln, scale=1.0 / 128, bias=EPS_RMS)
            p.inc(e.activation(p.ghm, p.ghm, AF.Exp, scale=-0.5), "rrc", 1)

        @blk.gpsimd
        def _(e):
            p.wait(e, "rrc")
            p.inc(e.dma_start(out=drms, in_=p.ghm), "rmd", 16)
            e.wait_ge(p.sem("rmd"), p.val("rmd"))

        @blk.sync
        def _(e):
            p.wait(e, "rmd")
            p.inc(e.dma_start(
                out=rstdb.rearrange("pp (v s) -> pp v s", v=NV),
                in_=drms.unsqueeze(0).broadcast_to([128, NV, TT])), "rmb", 16)
            e.wait_ge(p.sem("rmb"), p.val("rmb"))

        @blk.vector
        def _(e):
            p.wait(e, "rmb")
            e.tensor_mul(p.core[:], p.core[:], rstdb)
            p.inc(e.tensor_scalar_mul(p.core[:], p.core[:],
                                      p.nrmwt[:, 0:1]), "gt1", 1)

    with nc.Block() as blk:
        def evz(e, ot, bank):
            return e.activation(szv[:, ot * 512:(ot + 1) * 512], bank, AF.Silu)
        stream_proj(blk, swz, 32, KT,
                    lambda kt: p.hln[:, kt * 512:(kt + 1) * 512],
                    evz, "sp", "scalar")

    with nc.Block() as blk:
        @blk.vector
        def _(e):
            p.inc(e.tensor_mul(p.core[:], p.core[:], szv), "gt2", 1)

    with nc.Block() as blk:
        def evo(e, ot, bank):
            return e.tensor_copy(xbF[:, ot * 512:(ot + 1) * 512], bank)
        stream_proj(blk, swout, 16, 32,
                    lambda kt: p.core[:, kt * 512:(kt + 1) * 512],
                    evo, "sp", "vector")

    with nc.Block() as blk:
        ln_stats(blk, lambda kt: xbF[:, kt * 512:(kt + 1) * 512], "ln")
    with nc.Block() as blk:
        @blk.vector
        def _(e):
            for kt in range(16):
                sl = xbF[:, kt * 512:(kt + 1) * 512]
                e.tensor_sub(p.tmp512[:], sl, p.rbt[:, 0:512])
                p.inc(e.tensor_mul(sl, p.tmp512[:], p.rbt[:, 512:1024]),
                      "lcn", 1)

        @blk.gpsimd
        def _(e):
            p.wait(e, "lcn")
            src = xT[:, t0:t0 + TT].rearrange("(kt pp) s -> pp kt s", pp=128)
            p.inc(e.dma_start(
                out=p.xbuf[:].rearrange("pp (kt s) -> pp kt s", kt=16),
                in_=src, accum_op=ALU.add), "rsd", 16)
            e.wait_ge(p.sem("rsd"), p.val("rsd"))


def _phaseD(p, nc, t, t0, outT, swgu, swdn, pA, pB,
            stream_proj, ln_stats, ln_norm, xbF):
    qB = p.qkvp[:].bitcast(BF16)
    h1v = qB[:, 0:32768]
    dnv = p.core[:].bitcast(F32)[:, 0:8192]

    with nc.Block() as blk:
        ln_stats(blk, lambda kt: xbF[:, kt * 512:(kt + 1) * 512], "ln")
    with nc.Block() as blk:
        ln_norm(blk, lambda kt: xbF[:, kt * 512:(kt + 1) * 512],
                lambda kt: p.hln[:, kt * 512:(kt + 1) * 512], "lnn")

    with nc.Block() as blk:
        b_gul = [p.val("gul%d" % s) for s in range(3)]
        b_gur = p.val("gur")
        b_gsg = p.val("gsg")
        b_guh = p.val("guh")

        @blk.sync
        def _(e):
            for j in range(64):
                for half in range(2):
                    idx = 2 * j + half
                    for r in range(2):
                        blk_i = idx * 2 + r
                        if blk_i >= 3:
                            p.wait(e, "gur", b_gur + blk_i - 2)
                        p.inc(e.dma_start(
                            out=p.wbuf[:, (blk_i % 3) * 1024:
                                       (blk_i % 3 + 1) * 1024],
                            in_=swgu[:, blk_i * 1024:(blk_i + 1) * 1024]),
                            "gul%d" % (blk_i % 3), 16)

        @blk.tensor
        def _(e):
            for j in range(64):
                for half in range(2):
                    idx = 2 * j + half
                    if j >= 1:
                        if half == 0:
                            p.wait(e, "gsg", b_gsg + j)
                        else:
                            p.wait(e, "guh", b_guh + j)
                    bank = pA[half]
                    for r in range(2):
                        blk_i = idx * 2 + r
                        slot = blk_i % 3
                        p.wait(e, "gul%d" % slot,
                               b_gul[slot] + 16 * (blk_i // 3 + 1))
                        for jj in range(8):
                            kt = r * 8 + jj
                            ins = e.matmul(
                                bank,
                                p.wbuf[:, slot * 1024 + jj * 128:
                                       slot * 1024 + (jj + 1) * 128],
                                p.hln[:, kt * 512:(kt + 1) * 512],
                                start=(kt == 0), stop=(kt == 15))
                            if kt in (7, 15):
                                p.inc(ins, "gur", 1)

        @blk.scalar
        def _(e):
            for j in range(64):
                p.wait(e, "gur", b_gur + 2 * (2 * j + 1))
                p.inc(e.activation(p.ctmp[:, 0:512], pA[0][:], AF.Silu),
                      "gsg", 1)

        @blk.vector
        def _(e):
            for j in range(64):
                p.wait(e, "gur", b_gur + 2 * (2 * j + 2))
                e.tensor_copy(p.ctmp[:, 512:1024], pA[1][:])
                p.wait(e, "gsg", b_gsg + j + 1)
                p.inc(e.tensor_mul(h1v[:, j * 512:(j + 1) * 512],
                                   p.ctmp[:, 0:512], p.ctmp[:, 512:1024]),
                      "guh", 1)

    with nc.Block() as blk:
        def evd(e, ot, bank):
            return e.tensor_copy(dnv[:, ot * 512:(ot + 1) * 512], bank)
        stream_proj(blk, swdn, 16, 64,
                    lambda kt: h1v[:, kt * 512:(kt + 1) * 512],
                    evd, "sp", "vector")

    with nc.Block() as blk:
        ln_stats(blk, lambda kt: dnv[:, kt * 512:(kt + 1) * 512], "ln")
    with nc.Block() as blk:
        @blk.vector
        def _(e):
            for kt in range(16):
                sl = dnv[:, kt * 512:(kt + 1) * 512]
                e.tensor_sub(p.tmp512[:], sl, p.rbt[:, 0:512])
                e.tensor_mul(sl, p.tmp512[:], p.rbt[:, 512:1024])
                p.inc(e.tensor_add(sl, sl, xbF[:, kt * 512:(kt + 1) * 512]),
                      "len", 1)

        @blk.sync
        def _(e):
            p.wait(e, "len")
            dst = outT[:, t0:t0 + TT].rearrange("(kt pp) s -> pp kt s", pp=128)
            p.inc(e.dma_start(out=dst,
                              in_=dnv.rearrange("pp (kt s) -> pp kt s", kt=16)),
                  "outd", 16)
            e.wait_ge(p.sem("outd"), p.val("outd"))


def _consts():
    bf = ml_dtypes.bfloat16
    c = {}
    c["cones"] = np.ones((128, 1), bf)
    c["conesr"] = np.ones((1, 128), bf)
    cb = np.zeros((128, 63), np.float32)
    cb[:, 31] = 1.0
    c["cbandi"] = cb.astype(bf)
    c["cid128"] = np.eye(128, dtype=np.float32).astype(bf)
    c["cid128f"] = np.eye(128, dtype=np.float32)
    i16 = np.zeros((64, 16), np.float32)
    i16[0:16] = np.eye(16)
    i16[32:48] = np.eye(16)
    c["cid16f"] = i16
    ibd = np.zeros((128, 128), np.float32)
    ibd[:64, :64] = np.eye(64)
    ibd[64:, 64:] = np.eye(64)
    c["cibd"] = ibd.astype(bf)
    L = np.zeros((128, 128), np.float32)
    tri = np.triu(np.ones((64, 64), np.float32))
    L[:64, :64] = tri
    L[64:, 64:] = tri
    c["cLbd"] = L
    ti, tj = np.mgrid[0:64, 0:64]
    incl = (ti >= tj).astype(np.float32)
    strict = (ti > tj).astype(np.float32)
    c["cm_incl"] = np.tile(incl, (2, 1)).astype(bf)
    c["cm_strictneg"] = np.tile(-strict, (2, 1)).astype(bf)
    c["cmT_neg"] = np.tile(-(tj >= ti).astype(np.float32), (2, 1)).astype(bf)
    c["cmT_incl"] = np.tile((tj >= ti).astype(np.float32), (2, 1)).astype(bf)
    return c


def make_inputs(L0_attn_W_qkv, L0_attn_W_z, L0_attn_W_b, L0_attn_W_a,
                L0_attn_conv_w, L0_attn_dt_bias, L0_attn_A_log,
                L0_attn_norm_w, L0_attn_W_out, L0_ffn_w_gate_up,
                L0_ffn_w_down):
    convw = np.ascontiguousarray(
        np.asarray(L0_attn_conv_w, np.float32).reshape(OTQ, 128, 4)
        .transpose(1, 0, 2).reshape(128, OTQ * 4))
    def swz_(w, nr, no):
        w = np.asarray(w, np.float32).reshape(nr, 8, 128, no, 128)
        return np.ascontiguousarray(
            w.transpose(2, 3, 0, 1, 4).reshape(128, nr * no * 8 * 128))
    wguT = np.asarray(L0_ffn_w_gate_up, np.float32).T.reshape(
        2, 8, 128, 2, 64, 128)
    return dict(
        wqkv=swz_(L0_attn_W_qkv, 2, 64),
        wz=swz_(L0_attn_W_z, 2, 32),
        wb=np.ascontiguousarray(L0_attn_W_b, dtype=np.float32),
        wa=np.ascontiguousarray(L0_attn_W_a, dtype=np.float32),
        wout=swz_(L0_attn_W_out, 4, 16),
        wgu=np.ascontiguousarray(
            wguT.transpose(2, 4, 3, 0, 1, 5).reshape(128, 262144)),
        wdn=swz_(np.asarray(L0_ffn_w_down, np.float32).T, 8, 16),
        convw=convw,
        dtb=np.asarray(L0_attn_dt_bias, np.float32).reshape(1, NV),
        alog=np.asarray(L0_attn_A_log, np.float32).reshape(1, NV),
        nrmw=np.asarray(L0_attn_norm_w, np.float32).reshape(128, 1),
        **_consts(),
    )


def kernel(L0_attn_hidden_states, **kw):
    x = np.ascontiguousarray(L0_attn_hidden_states, dtype=np.float32)
    base = make_inputs(**kw)
    shards = [np.ascontiguousarray(x[b].T) for b in range(8)]
    nc, p = build_core_program()
    in_maps = [dict(base, xT=s) for s in shards]
    res = run_bass_kernel_spmd(nc, in_maps, core_ids=list(range(8)))
    return np.stack([np.ascontiguousarray(r["outT"].T) for r in res.results])
